# version 03
# baseline (speedup 1.0000x reference)
import numpy as np
import ml_dtypes
import concourse.bass as bass
import concourse.mybir as mybir
from concourse.bass_utils import run_bass_kernel_spmd
from contextlib import ExitStack

F32 = mybir.dt.float32
F16 = mybir.dt.float16
BF16 = mybir.dt.bfloat16
I32 = mybir.dt.int32
AF = mybir.ActivationFunctionType
OP = mybir.AluOpType
AX = mybir.AxisListType

P = 128
D = 2048
NDT = D // P
NH = 16
IO = 5632
NIT = IO // P
NPAIR = IO // P
SCH = 64
TC = 512
NTT = TC // P
GUW = D // 8
DNW = IO // 8
DQW = 512
NWIN = 5

ENGS = ("sync", "scalar", "vector", "gpsimd", "tensor")


class Tok:
    __slots__ = ("sem", "val", "weak")
    def __init__(self, sem, val, weak=False):
        self.sem = sem
        self.val = val
        self.weak = weak


class Prog:
    def __init__(self):
        self.ops = {e: [] for e in ENGS}
        self.count = {e: 0 for e in ENGS}
        self.buf = {}

    def op(self, eng, emit, reads=(), writes=(), extra=(), inc=1, sem=None, weak=False):
        reads = list(reads)
        writes = list(writes)
        for k in list(reads):
            if isinstance(k, tuple) and k and str(k[0]).startswith("pb"):
                reads.remove(k)
                writes.append(k)
        deps = list(extra)
        for k in reads:
            st = self.buf.get(k)
            if st and st[0] is not None:
                deps.append(st[0])
        for k in writes:
            st = self.buf.get(k)
            if st:
                if st[0] is not None:
                    deps.append(st[0])
                for r in st[1]:
                    assert not r.weak, f"WAR on weak (uncovered) PE reader of {k}"
                    deps.append(r)
        sem = sem or eng
        self.count[sem] = self.count.get(sem, 0) + inc
        tok = Tok(sem, self.count[sem], weak)
        self.ops[eng].append((deps, emit, inc, tok))
        for k in reads:
            self.buf.setdefault(k, [None, []])[1].append(tok)
        for k in writes:
            self.buf[k] = [tok, []]
        return tok

    def dma(self, eng, emit, reads=(), writes=(), extra=(), sem=None):
        assert sem is not None
        return self.op(eng, emit, reads, writes, extra, inc=16, sem=sem)

    def cover(self, keys, toks):
        for k in keys:
            st = self.buf.get(k)
            if not st:
                continue
            st[1] = [r for r in st[1] if not r.weak] + list(toks)

    def all_of(self, sem):
        return Tok(sem, self.count.get(sem, 0))

    def emit(self, block, sems):
        for eng in ENGS:
            ops = self.ops[eng]
            if not ops:
                continue

            def make_section(eng, ops):
                def section(h):
                    seen = {}
                    for deps, emit, inc, tok in ops:
                        need = {}
                        for d in deps:
                            if d.val > need.get(d.sem, 0):
                                need[d.sem] = d.val
                        for dsem in sorted(need):
                            dval = need[dsem]
                            if dsem == tok.sem:
                                dval = min(dval, tok.val - inc)
                            if dval <= seen.get(dsem, 0):
                                continue
                            h.wait_ge(sems[dsem], dval)
                            seen[dsem] = dval
                        emit(h).then_inc(sems[tok.sem], inc)
                return section
            getattr(block, eng)(make_section(eng, ops))


def cdiv(a, b):
    return (a + b - 1) // b


C2, C1, CM1, EPSLN, CM05, EPSQ, I128, C0, EPSF, I2048 = range(10)
CONST_VALS = [2.0, 1.0, -1.0, 1e-5, -0.5, 1.1920928955078125e-07,
              1.0 / 128.0, 0.0, 1e-6, 1.0 / 2048.0]


def build_nc(S):
    NCH = S // TC
    nc = bass.Bass("TRN2", target_bir_lowering=False)

    x_in = nc.dram_tensor("x_in", [S, D], F32, kind="ExternalInput").ap()
    win_blk = nc.dram_tensor("win_blk", [NH, NDT, P, 5 * P], F32, kind="ExternalInput").ap()
    wout_blk = nc.dram_tensor("wout_blk", [NDT, P, D], F32, kind="ExternalInput").ap()
    gup_pk = nc.dram_tensor("gup_pk", [GUW, 2 * IO], I32, kind="ExternalInput").ap()
    gup_sc = nc.dram_tensor("gup_sc", [GUW, 2 * IO], F16, kind="ExternalInput").ap()
    dn_pk = nc.dram_tensor("dn_pk", [DNW, D], I32, kind="ExternalInput").ap()
    dn_sc = nc.dram_tensor("dn_sc", [DNW, D], F16, kind="ExternalInput").ap()
    idm_in = nc.dram_tensor("idm", [P, P], BF16, kind="ExternalInput").ap()
    cst_in = nc.dram_tensor("cst", [P, 12], F32, kind="ExternalInput").ap()
    out_hbm = nc.dram_tensor("out", [S, D], F32, kind="ExternalOutput").ap()

    wgu_bf = nc.dram_tensor("wgu_bf", [D, 2 * IO], BF16, kind="Internal").ap()
    wdn_bf = nc.dram_tensor("wdn_bf", [IO, D], BF16, kind="Internal").ap()
    x1_hbm = nc.dram_tensor("x1_hbm", [S, D], F32, kind="Internal").ap()

    es = ExitStack()
    sb = lambda n, sh, dt: es.enter_context(nc.sbuf_tensor(n, sh, dt))
    ps = lambda n, sh, dt: es.enter_context(nc.psum_tensor(n, sh, dt))

    s_cst = sb("s_cst", [P, 12], F32)
    s_idm = sb("s_idm", [P, P], BF16)
    x_sb = [sb(f"x{t}", [P, D], F32) for t in range(NTT)]
    xn_sb = [sb(f"xn{t}", [P, D], BF16) for t in range(NTT)]
    xnT = [sb(f"xnT{d}", [P, TC], BF16) for d in range(NDT)]
    win_buf = [sb(f"winb{i}", [P, 4 * 5 * P], BF16) for i in range(NWIN)]
    wsm_buf = [sb(f"wsmb{i}", [P, 4 * TC], BF16) for i in range(4)]
    hy_sb = [sb(f"hy{i}", [P, TC], BF16) for i in range(NIT)]

    def attn_set(s):
        return dict(
            qf=sb(f"a_qf{s}", [P, TC], F32), vf=sb(f"a_vf{s}", [P, TC], F32),
            kc=sb(f"a_kc{s}", [P, TC], F32), as_=sb(f"a_as{s}", [P, TC], F32),
            gs=sb(f"a_gs{s}", [P, TC], F32), u=sb(f"a_u{s}", [P, TC], F32),
            a0=sb(f"a_a0{s}", [P, TC], F32), a1=sb(f"a_a1{s}", [P, TC], F32),
            pref=sb(f"a_pref{s}", [P, TC], F32),
            row=sb(f"s_row{s}", [1, TC], F32), row2=sb(f"s_row2{s}", [1, TC], F32))
    ATT = [attn_set(0), attn_set(1)]
    a_q2s = sb("a_q2s", [P, TC], BF16)
    s_mask = sb("s_mask", [P, TC], F32)
    s_ones_c = sb("s_ones_c", [P, 1], BF16)
    s_ones_r = sb("s_ones_r", [1, P], F32)
    s_state = sb("s_state", [P, NH], F32)
    ln_mean = [sb(f"lnm{t}", [P, 1], F32) for t in range(NTT)]
    ln_ssum = [sb(f"lns{t}", [P, 1], F32) for t in range(NTT)]
    ln_m2 = [sb(f"ln2_{t}", [P, 1], F32) for t in range(NTT)]
    ln_var = [sb(f"lnv{t}", [P, 1], F32) for t in range(NTT)]
    ln_rstd = [sb(f"lnr{t}", [P, 1], F32) for t in range(NTT)]
    f_ssum = sb("f_ssum", [P, NTT * 4], F32)
    f_tot = [sb(f"ftot{t}", [P, 1], F32) for t in range(NTT)]
    dq_pk = sb("dq_pk", [P, DQW], I32)
    dq_sc = sb("dq_sc", [P, DQW], F16)
    dq_em = sb("dq_em", [P, DQW], I32)
    dq_sg = sb("dq_sg", [P, DQW], I32)
    dq_r1 = sb("dq_r1", [P, DQW], F32)
    dq_o = sb("dq_o", [P, DQW], BF16)

    pb = [ps(f"pb{i}", [P, TC], F32) for i in range(5)]
    pb_misc = ps("pb_misc", [P, TC], F32)
    pb_tp = [ps(f"pbtp{i}", [P, 1024], BF16) for i in range(2)]

    DMA_SEMS = (["d_cst", "d_idm", "d_dqpk", "d_dqsc",
                 "d_wdeqg", "d_wdeqd", "d_x1h", "d_out"] +
                [f"d_x{i}" for i in range(NTT)] +
                [f"d_win{i}" for i in range(NWIN)] +
                [f"d_wsm{i}" for i in range(8)])
    sems = {e: es.enter_context(nc.semaphore(f"sem_{e}")) for e in ENGS}
    for snm in DMA_SEMS:
        sems[snm] = es.enter_context(nc.semaphore(snm))
    block = es.enter_context(nc.Block())

    pr = Prog()

    def cc(i, wp=P):
        return s_cst.ap()[0:wp, i:i + 1]

    with es:
        pr.dma("sync", lambda h: h.dma_start(out=s_cst[:], in_=cst_in),
               writes=[("cst",)], sem="d_cst")
        pr.dma("sync", lambda h: h.dma_start(out=s_idm[:], in_=idm_in),
               writes=[("idm",)], sem="d_idm")
        pr.op("vector", lambda v: v.memset(s_ones_c[:], 1.0), writes=[("onesc",)])
        pr.op("vector", lambda v: v.memset(s_ones_r[:], 1.0), writes=[("onesr",)])
        pr.op("vector", lambda v: v.memset(s_state[:], 0.0),
              writes=[("state", i) for i in range(NH)])
        pr.op("vector", lambda v: v.memset(s_mask[:], 1.0), writes=[("mask",)])
        pr.op("vector", lambda v: v.memset(s_mask.ap()[:, 0:TC:SCH], 0.0),
              reads=[("mask",)], writes=[("mask",)])

        def dequant_unit(pk_hbm, sc_hbm, dst, w0, wp, c0, cw, j):
            kpk, ksc = ("dqpk",), ("dqsc",)
            wsem = "d_wdeqg" if dst is wgu_bf else "d_wdeqd"
            if j == 0:
                pr.dma("sync", lambda h: h.dma_start(
                    out=dq_pk[0:wp, 0:cw], in_=pk_hbm[w0:w0 + wp, c0:c0 + cw]),
                    writes=[kpk], sem="d_dqpk")
                pr.dma("sync", lambda h: h.dma_start(
                    out=dq_sc[0:wp, 0:cw], in_=sc_hbm[w0:w0 + wp, c0:c0 + cw]),
                    writes=[ksc], sem="d_dqsc")
            sh = 22 - 4 * j
            if sh >= 0:
                pr.op("vector", lambda v: v.tensor_scalar(
                    dq_em[0:wp, 0:cw], dq_pk[0:wp, 0:cw], sh, 7 << 22,
                    OP.logical_shift_left, OP.bitwise_and), reads=[kpk], writes=[("dqem",)])
            else:
                pr.op("vector", lambda v: v.tensor_scalar(
                    dq_em[0:wp, 0:cw], dq_pk[0:wp, 0:cw], -sh, 7 << 22,
                    OP.logical_shift_right, OP.bitwise_and), reads=[kpk], writes=[("dqem",)])
            pr.op("vector", lambda v: v.tensor_scalar(
                dq_sg[0:wp, 0:cw], dq_pk[0:wp, 0:cw], 28 - 4 * j, -2147483648,
                OP.logical_shift_left, OP.bitwise_and), reads=[kpk], writes=[("dqsg",)])
            pr.op("vector", lambda v: v.tensor_scalar(
                dq_em[0:wp, 0:cw], dq_em[0:wp, 0:cw], 252 << 22, None, OP.add),
                reads=[("dqem",)], writes=[("dqem",)])
            pr.op("scalar", lambda s: s.activation(
                dq_r1[0:wp, 0:cw], dq_em.ap()[0:wp, 0:cw].bitcast(F32), AF.Relu,
                bias=cc(C1, wp), scale=cc(CM1, wp)),
                reads=[("dqem",), ("cst",)], writes=[("dqr1",)])
            pr.op("gpsimd", lambda g: g.tensor_tensor(
                dq_r1[0:wp, 0:cw], dq_em.ap()[0:wp, 0:cw].bitcast(F32), dq_r1[0:wp, 0:cw],
                OP.subtract), reads=[("dqem",), ("dqr1",)], writes=[("dqr1",)])
            pr.op("vector", lambda v: v.tensor_tensor(
                dq_r1.ap()[0:wp, 0:cw].bitcast(I32), dq_r1.ap()[0:wp, 0:cw].bitcast(I32),
                dq_sg[0:wp, 0:cw], OP.bitwise_or),
                reads=[("dqr1",), ("dqsg",)], writes=[("dqr1",)])
            pr.op("gpsimd", lambda g: g.tensor_tensor(
                dq_o[0:wp, 0:cw], dq_r1[0:wp, 0:cw], dq_sc[0:wp, 0:cw], OP.mult),
                reads=[("dqr1",), ksc], writes=[("dqo",)])
            pr.dma("gpsimd", lambda g: g.dma_start(
                out=dst.rearrange("(w e) n -> w e n", e=8)[w0:w0 + wp, j, c0:c0 + cw],
                in_=dq_o[0:wp, 0:cw]),
                reads=[("dqo",)], writes=[("wdeq", wsem)], sem=wsem)

        dq_units = []
        for wt in range(cdiv(GUW, P)):
            w0, wp = wt * P, min(P, GUW - wt * P)
            for ci in range(cdiv(2 * IO, DQW)):
                c0, cw = ci * DQW, min(DQW, 2 * IO - ci * DQW)
                for j in range(8):
                    dq_units.append((gup_pk, gup_sc, wgu_bf, w0, wp, c0, cw, j))
        n_gup_units = len(dq_units)
        for wt in range(cdiv(DNW, P)):
            w0, wp = wt * P, min(P, DNW - wt * P)
            for ci in range(cdiv(D, DQW)):
                c0, cw = ci * DQW, min(DQW, D - ci * DQW)
                for j in range(8):
                    dq_units.append((dn_pk, dn_sc, wdn_bf, w0, wp, c0, cw, j))
        dq_pos = [0]

        def emit_dq(n):
            stop = min(len(dq_units), dq_pos[0] + n)
            while dq_pos[0] < stop:
                dequant_unit(*dq_units[dq_pos[0]])
                dq_pos[0] += 1

        def emit_dq_upto(limit):
            while dq_pos[0] < limit:
                dequant_unit(*dq_units[dq_pos[0]])
                dq_pos[0] += 1

        def ln_pass():
            for tt in range(NTT):
                kx = ("x", tt)
                pr.op("vector", lambda v, tt=tt: v.reduce_sum(
                    ln_mean[tt][:], x_sb[tt][:], axis=AX.X), reads=[kx], writes=[("lnm", tt)])
                pr.op("vector", lambda v, tt=tt: v.tensor_scalar(
                    ln_mean[tt][:], ln_mean[tt][:], 1.0 / D, None, OP.mult),
                    reads=[("lnm", tt)], writes=[("lnm", tt)])
                pr.op("scalar", lambda s, tt=tt: s.activation(
                    xn_sb[tt][:], x_sb[tt][:], AF.Square, bias=cc(C0), scale=cc(C1),
                    accum_out=ln_ssum[tt][:]),
                    reads=[kx, ("cst",)], writes=[("xn", tt), ("lns", tt)])
                pr.op("scalar", lambda s, tt=tt: s.activation(
                    ln_m2[tt][:], ln_mean[tt][:], AF.Square, bias=cc(C0), scale=cc(C1)),
                    reads=[("lnm", tt), ("cst",)], writes=[("ln2", tt)])
                pr.op("vector", lambda v, tt=tt: v.scalar_tensor_tensor(
                    ln_var[tt][:], ln_ssum[tt][:], 1.0 / D, ln_m2[tt][:], OP.mult, OP.subtract),
                    reads=[("lns", tt), ("ln2", tt)], writes=[("lnv", tt)])
                pr.op("scalar", lambda s, tt=tt: s.activation(
                    ln_var[tt][:], ln_var[tt][:], AF.Ln, bias=cc(EPSLN), scale=cc(C1)),
                    reads=[("lnv", tt), ("cst",)], writes=[("lnv", tt)])
                pr.op("scalar", lambda s, tt=tt: s.activation(
                    ln_rstd[tt][:], ln_var[tt][:], AF.Exp, bias=cc(C0), scale=cc(CM05)),
                    reads=[("lnv", tt), ("cst",)], writes=[("lnr", tt)])
                pr.op("vector", lambda v, tt=tt: v.tensor_scalar(
                    xn_sb[tt][:], x_sb[tt][:], ln_mean[tt][:], ln_rstd[tt][:],
                    OP.subtract, OP.mult),
                    reads=[kx, ("lnm", tt), ("lnr", tt)], writes=[("xn", tt)])

        def transpose_pass():
            for tt in range(NTT):
                for dt in range(NDT):
                    par = (tt * NDT + dt) % 2
                    ktp = ("pbtp", par)
                    pr.op("tensor", lambda t, tt=tt, dt=dt, par=par: t.transpose(
                        pb_tp[par].ap()[:, 0:P], xn_sb[tt].ap()[:, dt * P:(dt + 1) * P],
                        s_idm[:]),
                        reads=[("xn", tt), ("idm",), ktp], writes=[ktp], weak=True)
                    t_ev = pr.op("scalar", lambda s, tt=tt, dt=dt, par=par: s.activation(
                        xnT[dt].ap()[:, tt * P:(tt + 1) * P], pb_tp[par].ap()[:, 0:P], AF.Copy),
                        reads=[ktp], writes=[("xnT", dt)])
                    pr.cover([("xn", tt)], [t_ev])

        for ch in range(NCH):
            base = ch * TC
            for tt in range(NTT):
                pr.dma("sync", lambda h, tt=tt, base=base: h.dma_start(
                    out=x_sb[tt][:], in_=x_in[base + tt * P: base + (tt + 1) * P, :]),
                    writes=[("x", tt)],
                    extra=[pr.all_of("d_x1h"), pr.all_of("d_out")], sem=f"d_x{tt}")
            ln_pass()
            transpose_pass()

            for ht in range(NH):
                A = ATT[ht % 2]
                sx = ht % 2
                for q in range(4):
                    slot = (ht * 4 + q) % NWIN
                    pr.dma("gpsimd", lambda h, ht=ht, q=q, slot=slot: h.dma_start(
                        out=win_buf[slot].ap().rearrange("p (a c) -> p a c", a=4),
                        in_=win_blk[ht, q * 4:(q + 1) * 4].transpose([1, 0, 2])),
                        writes=[("winb", slot)], sem=f"d_win{slot}")
                for q in range(4):
                    slot = (ht * 4 + q) % NWIN
                    def g_emit(t, q=q, slot=slot):
                        last = None
                        for dto in range(4):
                            dt = q * 4 + dto
                            coff = dto * 5 * P
                            for pj in range(5):
                                last = t.matmul(
                                    pb[pj][:],
                                    win_buf[slot].ap()[:, coff + pj * P: coff + (pj + 1) * P],
                                    xnT[dt][:],
                                    start=(dt == 0), stop=(dt == NDT - 1))
                        return last
                    pr.op("tensor", g_emit,
                          reads=[("winb", slot)] + [("xnT", q * 4 + i) for i in range(4)] +
                                [("pb", b) for b in range(5)],
                          writes=[("pb", b) for b in range(5)], weak=True)
                t_k = pr.op("scalar", lambda s, A=A: s.activation(
                    A["kc"][:], pb[1][:], AF.Tanh, bias=cc(C0), scale=cc(C1)),
                    reads=[("pb", 1), ("cst",)], writes=[("akc", sx)])
                t_v = pr.op("vector", lambda v, A=A: v.tensor_copy(A["vf"][:], pb[2][:]),
                            reads=[("pb", 2)], writes=[("avf", sx)])
                t_a = pr.op("scalar", lambda s, A=A: s.activation(
                    A["as_"][:], pb[3][:], AF.Sigmoid, bias=cc(C2), scale=cc(C1)),
                    reads=[("pb", 3), ("cst",)], writes=[("aas", sx)])
                pr.op("vector", lambda v, A=A: v.tensor_scalar(
                    A["as_"][:], A["as_"][:], 0.6, 0.9995, OP.max, OP.min),
                    reads=[("aas", sx)], writes=[("aas", sx)])
                t_qf = pr.op("vector", lambda v, A=A: v.tensor_copy(A["qf"][:], pb[0][:]),
                             reads=[("pb", 0)], writes=[("aqf", sx)])
                pr.op("vector", lambda v, A=A: v.tensor_tensor(
                    a_q2s[:], A["qf"][:], A["qf"][:], OP.mult),
                    reads=[("aqf", sx)], writes=[("aq2",)])
                t_g = pr.op("scalar", lambda s, A=A: s.activation(
                    A["gs"][:], pb[4][:], AF.Sigmoid, bias=cc(C0), scale=cc(C1)),
                    reads=[("pb", 4), ("cst",)], writes=[("ags", sx)])
                pr.cover([("winb", (ht * 4 + q) % NWIN) for q in range(4)] +
                         [("xnT", d) for d in range(NDT)],
                         [t_k, t_v, t_a, t_qf, t_g])
                pr.op("tensor", lambda t: t.matmul(
                    pb_misc.ap()[0:1, :], s_ones_c[:], a_q2s[:], start=True, stop=True),
                    reads=[("aq2",), ("onesc",), ("pbm",)], writes=[("pbm",)], weak=True)
                t_ln = pr.op("scalar", lambda s, A=A: s.activation(
                    A["row"][:], pb_misc.ap()[0:1, :], AF.Ln, bias=cc(EPSQ, 1), scale=cc(I128, 1)),
                    reads=[("pbm",), ("cst",)], writes=[("rowa", sx)])
                pr.cover([("aq2",), ("onesc",)], [t_ln])
                pr.op("scalar", lambda s, A=A: s.activation(
                    A["row2"][:], A["row"][:], AF.Exp, bias=cc(C0, 1), scale=cc(CM05, 1)),
                    reads=[("rowa", sx), ("cst",)], writes=[("rowb", sx)])
                pr.op("tensor", lambda t, A=A: t.matmul(
                    pb_misc[:], s_ones_r[:], A["row2"][:], start=True, stop=True),
                    reads=[("rowb", sx), ("onesr",), ("pbm",)], writes=[("pbm",)], weak=True)
                t_qn = pr.op("vector", lambda v, A=A: v.tensor_tensor(
                    A["qf"][:], A["qf"][:], pb_misc[:], OP.mult),
                    reads=[("aqf", sx), ("pbm",)], writes=[("aqf", sx)])
                pr.cover([("rowb", sx), ("onesr",)], [t_qn])
                pr.op("vector", lambda v, A=A: v.tensor_tensor(
                    A["u"][:], A["kc"][:], A["vf"][:], OP.mult),
                    reads=[("akc", sx), ("avf", sx)], writes=[("au", sx)])
                pr.op("vector", lambda v, A=A: v.tensor_tensor(
                    A["a0"][:], A["as_"][:], s_mask[:], OP.mult),
                    reads=[("aas", sx), ("mask",)], writes=[("aa0", sx)])
                pr.op("vector", lambda v, A=A: v.tensor_tensor(
                    A["a1"][:], A["as_"][:], A["a0"][:], OP.subtract),
                    reads=[("aas", sx), ("aa0", sx)], writes=[("aa1", sx)])
                pr.op("vector", lambda v, A=A: v.tensor_tensor_scan(
                    A["pref"][:], A["a0"][:], A["a1"][:], 1.0, OP.mult, OP.add),
                    reads=[("aa0", sx), ("aa1", sx)], writes=[("apref", sx)])
                pr.op("vector", lambda v, A=A: v.tensor_scalar(
                    A["pref"][:], A["pref"][:], 1e6, 1.0, OP.mult, OP.min),
                    reads=[("apref", sx)], writes=[("apref", sx)])
                pr.op("vector", lambda v, A=A: v.tensor_tensor(
                    A["pref"][:], A["u"][:], A["pref"][:], OP.mult),
                    reads=[("au", sx), ("apref", sx)], writes=[("apref", sx)])
                if ch == 0:
                    pr.op("vector", lambda v, A=A: v.tensor_tensor_scan(
                        A["a0"][:], A["as_"][:], A["pref"][:], 0.0, OP.mult, OP.add),
                        reads=[("aas", sx), ("apref", sx)], writes=[("aa0", sx)])
                else:
                    pr.op("vector", lambda v, A=A, ht=ht: v.tensor_tensor_scan(
                        A["a0"][:], A["as_"][:], A["pref"][:], s_state.ap()[:, ht:ht + 1],
                        OP.mult, OP.add),
                        reads=[("aas", sx), ("apref", sx), ("state", ht)], writes=[("aa0", sx)])
                if ch < NCH - 1:
                    pr.op("vector", lambda v, A=A, ht=ht: v.tensor_copy(
                        s_state.ap()[:, ht:ht + 1], A["a0"].ap()[:, TC - 1:TC]),
                        reads=[("aa0", sx)], writes=[("state", ht)])
                pr.op("vector", lambda v, A=A: v.tensor_tensor(
                    A["kc"][:], A["qf"][:], A["a0"][:], OP.mult),
                    reads=[("aqf", sx), ("aa0", sx)], writes=[("akc", sx)])
                pr.op("vector", lambda v, A=A: v.tensor_tensor(
                    A["kc"][:], A["kc"][:], A["vf"][:], OP.subtract),
                    reads=[("akc", sx), ("avf", sx)], writes=[("akc", sx)])
                pr.op("vector", lambda v, A=A: v.tensor_tensor(
                    A["kc"][:], A["kc"][:], A["gs"][:], OP.mult),
                    reads=[("akc", sx), ("ags", sx)], writes=[("akc", sx)])
                pr.op("vector", lambda v, A=A, ht=ht: v.tensor_tensor(
                    hy_sb[ht][:], A["kc"][:], A["vf"][:], OP.add),
                    reads=[("akc", sx), ("avf", sx)], writes=[("hy", ht)])
                emit_dq(max(1, n_gup_units // (NCH * NH)))

            for oc in range(4):
                for cg in range(4):
                    pr.dma("gpsimd", lambda h, oc=oc, cg=cg: h.dma_start(
                        out=wsm_buf[cg].ap().rearrange("p (a c) -> p a c", a=4),
                        in_=wout_blk[cg * 4:(cg + 1) * 4, :, oc * TC:(oc + 1) * TC].transpose([1, 0, 2])),
                        writes=[("wsmr", 2 * cg), ("wsmr", 2 * cg + 1)], sem=f"d_wsm{cg}")
                def wo_emit(t, oc=oc):
                    last = None
                    for ct in range(NDT):
                        for tt in range(NTT):
                            last = t.matmul(pb[tt][:],
                                            hy_sb[ct].ap()[:, tt * P:(tt + 1) * P],
                                            wsm_buf[ct // 4].ap()[:, (ct % 4) * TC:(ct % 4 + 1) * TC],
                                            start=(ct == 0), stop=(ct == NDT - 1))
                    return last
                pr.op("tensor", wo_emit,
                      reads=[("hy", ct) for ct in range(NDT)] +
                            [("wsmr", r) for r in range(8)] +
                            [("pb", tt) for tt in range(NTT)],
                      writes=[("pb", tt) for tt in range(NTT)], weak=True)
                ev = []
                for tt in range(NTT):
                    ev.append(pr.op("vector", lambda v, oc=oc, tt=tt: v.tensor_tensor(
                        x_sb[tt].ap()[:, oc * TC:(oc + 1) * TC], pb[tt][:],
                        x_sb[tt].ap()[:, oc * TC:(oc + 1) * TC], OP.add),
                        reads=[("pb", tt), ("x", tt)], writes=[("x", tt)]))
                pr.cover([("hy", ct) for ct in range(NDT)] +
                         [("wsmr", r) for r in range(8)], ev)
            for tt in range(NTT):
                pr.dma("sync", lambda h, tt=tt, base=base: h.dma_start(
                    out=x1_hbm[base + tt * P: base + (tt + 1) * P, :], in_=x_sb[tt][:]),
                    reads=[("x", tt)], writes=[("x1h", ch, tt)], sem="d_x1h")

        emit_dq_upto(n_gup_units)

        wgu_r = wgu_bf.rearrange("(dt p) (r io) -> dt p r io", p=P, io=IO)
        wdn_r = wdn_bf.rearrange("(it p) n -> it p n", p=P)
        for ch in range(NCH):
            base = ch * TC
            for tt in range(NTT):
                pr.dma("sync", lambda h, tt=tt, base=base: h.dma_start(
                    out=x_sb[tt][:], in_=x1_hbm[base + tt * P: base + (tt + 1) * P, :]),
                    reads=[("x1h", ch, tt)], writes=[("x", tt)],
                    extra=[pr.all_of("d_x1h"), pr.all_of("d_out")], sem=f"d_x{tt}")
            ln_pass()
            transpose_pass()

            for pair in range(NPAIR):
                pbg = pb[0] if pair % 2 == 0 else pb[2]
                pbu = pb[1] if pair % 2 == 0 else pb[3]
                kg = ("pb", 0 if pair % 2 == 0 else 2)
                ku = ("pb", 1 if pair % 2 == 0 else 3)
                for q in range(4):
                    slot = (pair * 4 + q) % NWIN
                    pr.dma("sync", lambda h, pair=pair, q=q, slot=slot: h.dma_start(
                        out=win_buf[slot].ap()[:, 0:512].rearrange("p (a c) -> p a c", a=4),
                        in_=wgu_r[q * 4:(q + 1) * 4, :, 0, pair * P:(pair + 1) * P].transpose(
                            [1, 0, 2])),
                        reads=[("wdeq", "d_wdeqg")], writes=[("winb", slot)],
                        sem=f"d_win{slot}")
                    pr.dma("sync", lambda h, pair=pair, q=q, slot=slot: h.dma_start(
                        out=win_buf[slot].ap()[:, 512:1024].rearrange("p (a c) -> p a c", a=4),
                        in_=wgu_r[q * 4:(q + 1) * 4, :, 1, pair * P:(pair + 1) * P].transpose(
                            [1, 0, 2])),
                        reads=[("wdeq", "d_wdeqg")], writes=[("winb", slot)],
                        sem=f"d_win{slot}")
                for q in range(4):
                    slot = (pair * 4 + q) % NWIN
                    def gu_mm(t, q=q, slot=slot, pbg=pbg, pbu=pbu):
                        last = None
                        for dto in range(4):
                            dt = q * 4 + dto
                            last = t.matmul(pbg[:],
                                            win_buf[slot].ap()[:, dto * P:(dto + 1) * P],
                                            xnT[dt][:], start=(dt == 0), stop=(dt == NDT - 1))
                            last = t.matmul(pbu[:],
                                            win_buf[slot].ap()[:, 512 + dto * P: 512 + (dto + 1) * P],
                                            xnT[dt][:], start=(dt == 0), stop=(dt == NDT - 1))
                        return last
                    pr.op("tensor", gu_mm,
                          reads=[("winb", slot)] + [("xnT", q * 4 + i) for i in range(4)] +
                                [kg, ku],
                          writes=[kg, ku], weak=True)
                t_si = pr.op("scalar", lambda s, pbg=pbg: s.activation(
                    a_q2s[:], pbg[:], AF.Silu, bias=cc(C0), scale=cc(C1)),
                    reads=[kg, ("cst",)], writes=[("aq2",)])
                t_h = pr.op("vector", lambda v, pair=pair, pbu=pbu: v.tensor_tensor(
                    hy_sb[pair][:], a_q2s[:], pbu[:], OP.mult),
                    reads=[("aq2",), ku], writes=[("hy", pair)])
                pr.cover([("winb", (pair * 4 + q) % NWIN) for q in range(4)] +
                         [("xnT", d) for d in range(NDT)], [t_si, t_h])
                emit_dq(max(1, (len(dq_units) - n_gup_units) // (2 * NPAIR)))
            emit_dq_upto(len(dq_units))

            for hc in range(4):
                since = set()
                for ip in range(NIT // 2):
                    reg = ip % 8
                    tb, half = reg % 4, reg // 4
                    pr.dma("sync", lambda h, hc=hc, ip=ip, tb=tb, half=half: h.dma_start(
                        out=wsm_buf[tb].ap()[:, half * 2 * TC: (half + 1) * 2 * TC].rearrange(
                            "p (a c) -> p a c", a=2),
                        in_=wdn_r[2 * ip:2 * ip + 2, :, hc * TC:(hc + 1) * TC].transpose([1, 0, 2])),
                        reads=[("wdeq", "d_wdeqd")], writes=[("wsmr", reg)], sem=f"d_wsm{tb}")
                    def dn_mm(t, ip=ip, tb=tb, half=half):
                        last = None
                        for k in range(2):
                            it = 2 * ip + k
                            for tt in range(NTT):
                                last = t.matmul(
                                    pb[tt][:],
                                    hy_sb[it].ap()[:, tt * P:(tt + 1) * P],
                                    wsm_buf[tb].ap()[:, (half * 2 + k) * TC:(half * 2 + k + 1) * TC],
                                    start=(it == 0), stop=(it == NIT - 1))
                        return last
                    pr.op("tensor", dn_mm,
                          reads=[("hy", 2 * ip), ("hy", 2 * ip + 1), ("wsmr", reg)] +
                                [("pb", tt) for tt in range(NTT)],
                          writes=[("pb", tt) for tt in range(NTT)], weak=True)
                    since.add(reg)
                    if (ip + 1) % 8 == 0 and ip != NIT // 2 - 1:
                        t_dr = pr.op("tensor", lambda t: t.drain(fusable=False))
                        pr.cover([("wsmr", r) for r in since], [t_dr])
                        since = set()
                ev = []
                for tt in range(NTT):
                    t_cp = pr.op("vector", lambda v, hc=hc, tt=tt: v.tensor_copy(
                        xn_sb[tt].ap()[:, hc * TC:(hc + 1) * TC], pb[tt][:]),
                        reads=[("pb", tt)], writes=[("xn", tt)])
                    t_sq = pr.op("scalar", lambda s, hc=hc, tt=tt: s.activation(
                        a_q2s[:], pb[tt][:], AF.Square, bias=cc(C0), scale=cc(C1),
                        accum_out=f_ssum.ap()[:, tt * 4 + hc: tt * 4 + hc + 1]),
                        reads=[("pb", tt), ("cst",)],
                        writes=[("aq2",), ("fss", tt, hc)])
                    ev += [t_cp, t_sq]
                pr.cover([("hy", it) for it in range(NIT)] +
                         [("wsmr", r) for r in range(8)], ev)
            for tt in range(NTT):
                pr.op("vector", lambda v, tt=tt: v.reduce_sum(
                    f_tot[tt][:], f_ssum.ap()[:, tt * 4:(tt + 1) * 4], axis=AX.X),
                    reads=[("fss", tt, h2) for h2 in range(4)], writes=[("ftot", tt)])
                pr.op("scalar", lambda s, tt=tt: s.activation(
                    f_tot[tt][:], f_tot[tt][:], AF.Ln, bias=cc(EPSF), scale=cc(I2048)),
                    reads=[("ftot", tt), ("cst",)], writes=[("ftot", tt)])
                pr.op("scalar", lambda s, tt=tt: s.activation(
                    f_tot[tt][:], f_tot[tt][:], AF.Exp, bias=cc(C0), scale=cc(CM05)),
                    reads=[("ftot", tt), ("cst",)], writes=[("ftot", tt)])
                pr.op("vector", lambda v, tt=tt: v.scalar_tensor_tensor(
                    x_sb[tt][:], xn_sb[tt][:], f_tot[tt][:], x_sb[tt][:], OP.mult, OP.add),
                    reads=[("xn", tt), ("ftot", tt), ("x", tt)], writes=[("x", tt)])
                pr.dma("sync", lambda h, tt=tt, base=base: h.dma_start(
                    out=out_hbm[base + tt * P: base + (tt + 1) * P, :], in_=x_sb[tt][:]),
                    reads=[("x", tt)], writes=[("outh", ch, tt)], sem="d_out")

        pr.op("sync", lambda h: h.nop(), extra=[pr.all_of("d_out")])
        pr.emit(block, sems)
    return nc


def _host_layouts(w_in, w_out, gup_packed, gup_scales, dn_packed, dn_scales):
    wT = np.ascontiguousarray(w_in.T)
    win_blk = np.ascontiguousarray(
        wT.reshape(NDT, P, 5, NH, P).transpose(3, 0, 1, 2, 4).reshape(NH, NDT, P, 5 * P))
    wout_blk = np.ascontiguousarray(w_out.T.reshape(NDT, P, D))
    gup_pk = np.ascontiguousarray(gup_packed.T)
    gup_sc = np.ascontiguousarray(np.repeat(gup_scales, 16, axis=0))
    dn_pk = np.ascontiguousarray(dn_packed.T)
    dn_sc = np.ascontiguousarray(np.repeat(dn_scales, 16, axis=0))
    idm = np.eye(P, dtype=np.float32).astype(ml_dtypes.bfloat16)
    cst = np.broadcast_to(np.array(CONST_VALS + [0.0, 0.0], np.float32), (P, 12)).copy()
    return win_blk, wout_blk, gup_pk, gup_sc, dn_pk, dn_sc, idm, cst


def kernel(L0_attn_x, L0_attn_w_in, L0_attn_w_out, L0_ffn_gate_up_packed,
           L0_ffn_gate_up_scales, L0_ffn_down_packed, L0_ffn_down_scales):
    x = np.ascontiguousarray(np.asarray(L0_attn_x, dtype=np.float32))
    B, S, Dm = x.shape
    assert Dm == D
    win_blk, wout_blk, gup_pk, gup_sc, dn_pk, dn_sc, idm, cst = _host_layouts(
        np.asarray(L0_attn_w_in, np.float32), np.asarray(L0_attn_w_out, np.float32),
        np.asarray(L0_ffn_gate_up_packed, np.int32),
        np.asarray(L0_ffn_gate_up_scales, np.float16),
        np.asarray(L0_ffn_down_packed, np.int32),
        np.asarray(L0_ffn_down_scales, np.float16))
    nc = build_nc(S)
    shared = dict(win_blk=win_blk, wout_blk=wout_blk, gup_pk=gup_pk, gup_sc=gup_sc,
                  dn_pk=dn_pk, dn_sc=dn_sc, idm=idm, cst=cst)
    in_maps = [dict(x_in=np.ascontiguousarray(x[c]), **shared) for c in range(B)]
    res = run_bass_kernel_spmd(nc, in_maps, core_ids=list(range(B)))
    out = np.stack([np.asarray(res.results[c]["out"], dtype=np.float32) for c in range(B)])
    return out
